# version 08
# baseline (speedup 1.0000x reference)
import numpy as np
import concourse.bass as bass
import concourse.mybir as mybir
from concourse.bass_utils import run_bass_kernel_spmd

F32 = mybir.dt.float32
BF16 = mybir.dt.bfloat16
AF = mybir.ActivationFunctionType
ALU = mybir.AluOpType
AX = mybir.AxisListType

D = 2048
T = 2048
NCK = 16
NQ = 4
TQ = 512
LCH = 256
NE = 32
NI = 6
NST = 16
NR = 64


class _Dummy:
    def __getattr__(self, k):
        return self

    def __call__(self, *a, **k):
        return self

    def __getitem__(self, k):
        return self


class Sem:
    def __init__(self, ctx, name):
        self.name = name
        self.h = ctx.nc.alloc_semaphore(name=name) if ctx.real else _Dummy()
        self.n = 0


class Ctx:
    def __init__(self, nc, tbl=None):
        self.real = nc is not None
        self.nc = nc if self.real else _Dummy()
        self.tbl = {} if tbl is None else tbl
        self.fill = tbl is None
        names = ["T", "V", "S", "SP", "SXx", "SXM", "SWB", "SBC", "SBS",
                 "SWT", "SO", "SR", "SS0", "SS1", "SWg", "SWu"]
        names += [f"SDg{r}" for r in range(8)] + [f"SDs{r}" for r in range(8)]
        names += [f"SL{i}" for i in range(4)]
        self.sems = {k: Sem(self, k) for k in names}
        self.wm = {}
        self.eng = {"T": self.nc.tensor, "V": self.nc.vector,
                    "S": self.nc.scalar, "sync": self.nc.sync,
                    "gpsimd": self.nc.gpsimd}
        for k in self.sems:
            if k not in ("T", "V", "S"):
                self.eng.setdefault(k, self.nc.sync)
        self.scb = [None, None]
        self.sc_i = 0

    def inc(self, semname, instr, k=1, key=None):
        s = self.sems[semname]
        if self.real:
            instr.then_inc(s.h, k)
        s.n += k
        if key is not None:
            self.mark(semname, key)
        return s.n

    def mark(self, semname, key):
        if self.fill:
            self.tbl[key] = (semname, self.sems[semname].n)
        else:
            assert self.tbl.get(key) == (semname, self.sems[semname].n), (
                f"pass mismatch at {key}: pass1={self.tbl.get(key)} "
                f"pass2=({semname}, {self.sems[semname].n})")

    def wait(self, engine, key):
        if self.fill:
            return
        if key not in self.tbl:
            raise KeyError(f"missing milestone {key}")
        semname, v = self.tbl[key]
        if v <= 0:
            return
        eng = self.eng[engine]
        wmk = (id(eng), semname)
        if self.wm.get(wmk, -1) >= v:
            return
        self.wm[wmk] = v
        if self.real:
            eng.wait_ge(self.sems[semname].h, v)

    def fence(self, semname, ins):
        n = self.inc(semname, ins)
        if self.real:
            self.eng[semname].wait_ge(self.sems[semname].h, n)

    def dma(self, qeng, sem, out, in_, key=None, **kw):
        eng = self.eng[qeng]
        sm = self.sems[sem]
        if sm.n > 0 and self.real:
            eng.wait_ge(sm.h, sm.n)
        ins = eng.dma_start(out=out, in_=in_, **kw)
        self.inc(sem, ins, 16, key=key)

    def sc_acquire(self, producer_eng):
        i = self.sc_i % 2
        self.sc_i += 1
        if self.scb[i] is not None:
            self.wait(producer_eng, self.scb[i])
        return i

    def sc_release(self, i, consumer_key):
        self.scb[i] = consumer_key


def _build(ctx, dbg=False):
    nc = ctx.nc
    x_d = nc.dram_tensor("x", [T, D], F32, kind="ExternalInput").ap()
    wx_d = nc.dram_tensor("wx", [96, D], F32, kind="ExternalInput").ap()
    wdt_d = nc.dram_tensor("wdt", [D, NR], F32, kind="ExternalInput").ap()
    bdt_d = nc.dram_tensor("bdt_in", [16, 128], F32, kind="ExternalInput").ap()
    alog_d = nc.dram_tensor("alog", [D, NST], F32, kind="ExternalInput").ap()
    dvec_d = nc.dram_tensor("dvec_in", [16, 128], F32, kind="ExternalInput").ap()
    rw_d = nc.dram_tensor("rw", [NE, D], F32, kind="ExternalInput").ap()
    wg_d = nc.dram_tensor("wgt", [NE, D, 768], F32, kind="ExternalInput").ap()
    wu_d = nc.dram_tensor("wup", [NE, D, 768], F32, kind="ExternalInput").ap()
    wd_d = nc.dram_tensor("wdn", [NE, 768, D], F32, kind="ExternalInput").ap()
    id_d = nc.dram_tensor("ident", [128, 128], F32, kind="ExternalInput").ap()
    out_d = nc.dram_tensor("out", [T, D], F32, kind="ExternalOutput").ap()
    sfm_d = nc.dram_tensor("sfm", [D, T], BF16, kind="Internal").ap()
    cwg_d = nc.dram_tensor("cwg", [NE, 128, NCK * 768], BF16, kind="Internal").ap()
    cwu_d = nc.dram_tensor("cwu", [NE, 128, NCK * 768], BF16, kind="Internal").ap()
    cwd_d = nc.dram_tensor("cwd", [NE, NCK, 128, NI * 128], BF16, kind="Internal").ap()
    bcs_d = nc.dram_tensor("bcs", [32, LCH], BF16, kind="Internal").ap()
    wts_d = nc.dram_tensor("wts", [NE, TQ], BF16, kind="Internal").ap()
    if dbg:
        dbg_xt = nc.dram_tensor("dbg_xt", [128, 2 * D], F32, kind="ExternalOutput").ap()
        dbg_mb = nc.dram_tensor("dbg_mb", [128, 8 * LCH], F32, kind="ExternalOutput").ap()
        dbg_bc = nc.dram_tensor("dbg_bc", [128, 2 * NST * LCH], BF16, kind="ExternalOutput").ap()
        dbg_xp = nc.dram_tensor("dbg_xp", [96, LCH], F32, kind="ExternalOutput").ap()
        dbg_xnbf = nc.dram_tensor("dbg_xnbf", [128, NCK * TQ], BF16, kind="ExternalOutput").ap()
        dbg_s = nc.dram_tensor("dbg_s", [D, T], F32, kind="ExternalOutput").ap()
        dbg_acc = nc.dram_tensor("dbg_acc", [128, NCK * TQ], F32, kind="ExternalOutput").ap()
        dbg_w = nc.dram_tensor("dbg_w", [128, 4 * 32], F32, kind="ExternalOutput").ap()

    A = lambda n, shp, dt: nc.alloc_sbuf_tensor(n, shp, dt).ap()
    ident = A("identS", [128, 128], F32)
    cpool = A("cpool", [128, 4], F32)
    WxT = A("WxT", [128, NCK * 96], BF16)
    WdtT = A("WdtT", [64, NCK * 128], BF16)
    rwT = A("rwT", [128, NCK * 32], BF16)
    negP = A("negP", [128, NCK * NST], F32)
    bdt = A("bdtS", [128, 16], F32)
    Dv = A("DvS", [128, 16], F32)
    x_t = A("x_t", [128, D], F32)
    xnbf = A("xnbf", [128, NCK * TQ], BF16)
    acc = A("acc", [128, NCK * TQ], F32)
    wgS = A("wgS", [128, NCK * 768], BF16)
    wuS = A("wuS", [128, NCK * 768], BF16)
    wdR = A("wdR", [128, 8 * 768], BF16)
    h_e = A("h_e", [128, 2 * NI * TQ], BF16)
    sgb = A("sgb", [128, 4 * TQ], BF16)
    upw = A("upw", [128, 2 * TQ], F32)
    wbc = A("wbc", [128, 2 * TQ], BF16)
    Bbc = A("Bbc", [128, NST * LCH], BF16)
    Cbc = A("Cbc", [128, NST * LCH], BF16)
    dtc = A("dtc", [128, LCH], F32)
    dtxc = A("dtxc", [128, LCH], F32)
    sp0 = A("sp0", [128, LCH], F32)
    sp1 = A("sp1", [128, LCH], F32)
    dAr = A("dAr", [128, 8 * LCH], F32)
    dBr = A("dBr", [128, 2 * LCH], F32)
    hsc = A("hsc", [128, 2 * LCH], F32)
    ytmp = A("ytmp", [128, LCH], F32)
    y_c = A("y_c", [128, LCH], F32)
    hst = A("hst", [128, 256], F32)
    s_c = A("s_c", [128, 2 * LCH], BF16)
    xproj = A("xproj", [96, LCH], F32)
    xprojbc = A("xprojbc", [96, LCH], BF16)
    xprojb = A("xprojb", [64, LCH], BF16)
    probs = A("probs", [128, 4 * 32], F32)
    top8 = A("top8", [128, 4 * 8], F32)
    rcol = A("rcol", [128, 12], F32)
    msk = A("msk", [128, 32], F32)
    w_tm = A("w_tm", [128, 4 * 32], F32)
    wTq = A("wTq", [32, TQ], BF16)
    s_sl = A("s_sl", [128, NCK * 128], BF16)
    sg_c = A("sg_c", [128, 2 * 128], F32)
    x_tm = A("x_tm", [128, D], F32)
    stat = A("stat", [128, 12], F32)

    P = lambda n, shp: nc.alloc_psum_tensor(n, shp, F32).ap()
    ps_g = [P("ps_gA", [128, 512]), P("ps_gB", [128, 512])]
    ps_u = [P("ps_uA", [128, 512]), P("ps_uB", [128, 512])]
    ps_dn = P("ps_dn", [128, 1024])
    ps_sc = P("ps_sc", [128, 1024])
    ps_cb = [ps_g[0], ps_u[0], ps_g[1], ps_u[1]]

    TT, VV, SS = ctx.eng["T"], ctx.eng["V"], ctx.eng["S"]

    ctx.dma("sync", "SP", ident, id_d, key="ld_id")
    ctx.dma("sync", "SP", x_t[0:96, 0:D], wx_d, key="ld_wx")
    ctx.dma("sync", "SP", negP.rearrange("p (c n) -> p c n", c=NCK),
            alog_d.rearrange("(c p) n -> p c n", p=128),
            key="ld_alog", allow_slow_non_contiguous=True)
    VV.memset(cpool[:, 0:1], 1e-5)
    VV.memset(cpool[:, 1:2], 1.0)
    VV.memset(cpool[:, 2:3], 0.125)
    VV.memset(cpool[:, 3:4], 1.0 / D)
    vi = VV.memset(hst, 0.0)
    ctx.inc("V", vi, key="consts")
    ctx.wait("S", "ld_alog")
    si = SS.activation(negP, negP, AF.Exp)
    ctx.inc("S", si, key="neg_exp")
    ctx.wait("V", "neg_exp")
    vi = VV.tensor_scalar_mul(negP, negP, -1.0)
    ctx.inc("V", vi, key="negP")

    ctx.wait("T", "ld_id")
    ctx.wait("T", "ld_wx")
    for k in range(NCK):
        b = ctx.sc_acquire("T")
        ti = TT.transpose(ps_sc[:, b * 512:b * 512 + 96],
                          x_t[0:96, k * 128:(k + 1) * 128], ident[0:96, 0:96])
        ctx.inc("T", ti, key=f"pt_wx{k}")
        ctx.wait("S", f"pt_wx{k}")
        si = SS.copy(WxT[:, k * 96:(k + 1) * 96], ps_sc[:, b * 512:b * 512 + 96])
        ctx.inc("S", si, key=f"pe_wx{k}")
        ctx.sc_release(b, f"pe_wx{k}")
    ctx.wait("SP", "pt_wx15")
    ctx.dma("sync", "SP", x_t[:, 0:NCK * NR].rearrange("p (c r) -> p c r", c=NCK),
            wdt_d.rearrange("(c p) r -> p c r", p=128), key="ld_wdt")
    ctx.dma("sync", "SP", x_t[0:16, 1024:1024 + 128], bdt_d, key="ld_bdt")
    ctx.dma("sync", "SP", x_t[0:16, 1152:1152 + 128], dvec_d, key="ld_dv")
    ctx.wait("T", "ld_wdt")
    ctx.wait("T", "ld_bdt")
    ctx.wait("T", "ld_dv")
    for c in range(NCK):
        b = ctx.sc_acquire("T")
        ti = TT.transpose(ps_sc[0:64, b * 512:b * 512 + 128],
                          x_t[:, c * 64:(c + 1) * 64], ident)
        ctx.inc("T", ti, key=f"pt_wdt{c}")
        ctx.wait("S", f"pt_wdt{c}")
        si = SS.copy(WdtT[:, c * 128:(c + 1) * 128],
                     ps_sc[0:64, b * 512:b * 512 + 128])
        ctx.inc("S", si, key=f"pe_wdt{c}")
        ctx.sc_release(b, f"pe_wdt{c}")
    b = ctx.sc_acquire("T")
    ti = TT.transpose(ps_sc[0:128, b * 512:b * 512 + 16],
                      x_t[0:16, 1024:1024 + 128], ident[0:16, 0:16])
    ti = TT.transpose(ps_sc[0:128, b * 512 + 16:b * 512 + 32],
                      x_t[0:16, 1152:1152 + 128], ident[0:16, 0:16])
    ctx.inc("T", ti, key="pt_bd")
    ctx.wait("S", "pt_bd")
    si = SS.copy(bdt, ps_sc[:, b * 512:b * 512 + 16])
    si = SS.copy(Dv, ps_sc[:, b * 512 + 16:b * 512 + 32])
    ctx.inc("S", si, key="pe_bd")
    ctx.sc_release(b, "pe_bd")
    ctx.wait("SP", "pt_bd")
    ctx.dma("sync", "SP", x_t[0:32, 0:D], rw_d, key="ld_rw")
    ctx.wait("T", "ld_rw")
    for k in range(NCK):
        b = ctx.sc_acquire("T")
        ti = TT.transpose(ps_sc[:, b * 512:b * 512 + 32],
                          x_t[0:32, k * 128:(k + 1) * 128], ident[0:32, 0:32])
        ctx.inc("T", ti, key=f"pt_rw{k}")
        ctx.wait("S", f"pt_rw{k}")
        si = SS.copy(rwT[:, k * 32:(k + 1) * 32], ps_sc[:, b * 512:b * 512 + 32])
        ctx.inc("S", si, key=f"pe_rw{k}")
        ctx.sc_release(b, f"pe_rw{k}")
    ctx.mark("S", "prologue_s")
    ctx.mark("T", "prologue_t")

    def chunk_transposes(q, ch):
        if q > 0:
            ctx.wait("S", f"xnbf_free_t/{q - 1}")
            ctx.wait("S", f"sdone/{q - 1}/1/15")
        for j in range(2):
            tile = q * 4 + ch * 2 + j
            prev = (q, ch, j - 1) if j > 0 else (
                (q, ch - 1, 1) if ch > 0 else ((q - 1, 1, 1) if q > 0 else None))
            if prev is not None:
                ctx.wait("sync", f"tp/{prev[0]}/{prev[1]}/{prev[2]}/15")
            else:
                ctx.wait("sync", "pt_rw15")
            ctx.dma("sync", "SXx", x_t,
                    x_d[tile * 128:(tile + 1) * 128, :], key=f"xl/{q}/{ch}/{j}")
            ctx.wait("S", f"xl/{q}/{ch}/{j}")
            if q > 0 and (ch, j) == (0, 0):
                ctx.wait("S", f"scan/{q - 1}/1/15/15")
            xt = x_t
            SS.activation(dAr, xt, AF.Square, accum_out=stat[:, 0:1])
            si = SS.activation(dAr, xt, AF.Copy, accum_out=stat[:, 1:2])
            ctx.inc("S", si, key=f"lns/{q}/{ch}/{j}")
            ctx.wait("V", f"lns/{q}/{ch}/{j}")
            ctx.wait("V", "consts")
            ctx.fence("V", VV.tensor_mul(stat[:, 2:3], stat[:, 1:2], cpool[:, 3:4]))
            ctx.fence("V", VV.tensor_mul(stat[:, 3:4], stat[:, 2:3], stat[:, 2:3]))
            ctx.fence("V", VV.tensor_mul(stat[:, 4:5], stat[:, 0:1], cpool[:, 3:4]))
            vi = VV.tensor_sub(stat[:, 5:6], stat[:, 4:5], stat[:, 3:4])
            ctx.inc("V", vi, key=f"var/{q}/{ch}/{j}")
            ctx.wait("S", f"var/{q}/{ch}/{j}")
            si = SS.activation(stat[:, 6:7], stat[:, 5:6], AF.Sqrt,
                               bias=cpool[:, 0:1])
            ctx.inc("S", si, key=f"sqv/{q}/{ch}/{j}")
            ctx.wait("V", f"sqv/{q}/{ch}/{j}")
            ctx.fence("V", VV.reciprocal(stat[:, 6:7], stat[:, 6:7]))
            vi = VV.tensor_scalar(xt, xt, stat[:, 2:3], stat[:, 6:7],
                                  ALU.subtract, ALU.mult)
            ctx.inc("V", vi, key=f"ln/{q}/{ch}/{j}")
            for c in range(NCK):
                bk = ctx.sc_acquire("T")
                ctx.wait("T", f"ln/{q}/{ch}/{j}")
                ti = TT.transpose(ps_sc[:, bk * 512:bk * 512 + 128],
                                  x_t[:, c * 128:(c + 1) * 128], ident)
                ctx.inc("T", ti, key=f"tp/{q}/{ch}/{j}/{c}")
                ctx.wait("S", f"tp/{q}/{ch}/{j}/{c}")
                col = c * TQ + ch * LCH + j * 128
                si = SS.copy(xnbf[:, col:col + 128],
                             ps_sc[:, bk * 512:bk * 512 + 128])
                ctx.inc("S", si, key=f"xnbf/{q}/{ch}/{j}/{c}")
                ctx.sc_release(bk, f"xnbf/{q}/{ch}/{j}/{c}")

    def chunk_aux(q, ch):
        b = ctx.sc_acquire("T")
        ctx.wait("T", f"xnbf/{q}/{ch}/1/15")
        for k in range(NCK):
            ti = TT.matmul(ps_sc[0:96, b * 512:b * 512 + LCH],
                           WxT[:, k * 96:(k + 1) * 96],
                           xnbf[:, k * TQ + ch * LCH:k * TQ + (ch + 1) * LCH],
                           start=(k == 0), stop=(k == NCK - 1))
        ctx.inc("T", ti, key=f"xpm/{q}/{ch}")
        ctx.wait("S", f"xpm/{q}/{ch}")
        if ch == 1:
            ctx.wait("S", f"dtm/{q}/0/15")
        elif q > 0:
            ctx.wait("S", f"dtm/{q - 1}/1/15")
        SS.copy(xproj, ps_sc[0:96, b * 512:b * 512 + LCH])
        SS.copy(xprojb, ps_sc[0:64, b * 512:b * 512 + LCH])
        si = SS.copy(xprojbc[64:96, :], ps_sc[64:96, b * 512:b * 512 + LCH])
        ctx.inc("S", si, key=f"xproj/{q}/{ch}")
        ctx.sc_release(b, f"xproj/{q}/{ch}")
        if ch == 1:
            ctx.wait("sync", f"bc/{q}/0")
        elif q > 0:
            ctx.wait("sync", f"bc/{q - 1}/1")
        ctx.wait("sync", f"xproj/{q}/{ch}")
        ctx.dma("sync", "SBS", bcs_d, xprojbc[64:96, :], key=f"bcs/{q}/{ch}")
        if ch == 0 and q > 0:
            ctx.wait("sync", f"unit/{q - 1}/1/15")
        if ch == 1:
            ctx.wait("sync", f"unit/{q}/0/15")
        ctx.wait("sync", f"bcs/{q}/{ch}")
        ctx.dma("sync", "SBC", Bbc.rearrange("p (n l) -> p n l", n=NST),
                bcs_d[0:16, :].partition_broadcast(128))
        ctx.dma("sync", "SBC", Cbc.rearrange("p (n l) -> p n l", n=NST),
                bcs_d[16:32, :].partition_broadcast(128), key=f"bc/{q}/{ch}")

    def router(q):
        b = ctx.sc_acquire("T")
        ctx.wait("T", f"xnbf/{q}/1/1/15")
        ctx.wait("T", "prologue_s")
        for tt in range(4):
            for k in range(NCK):
                ti = TT.matmul(
                    ps_sc[:, b * 512 + tt * 32:b * 512 + (tt + 1) * 32],
                    xnbf[:, k * TQ + tt * 128:k * TQ + (tt + 1) * 128],
                    rwT[:, k * 32:(k + 1) * 32],
                    start=(k == 0), stop=(k == NCK - 1))
        ctx.inc("T", ti, key=f"lg/{q}")
        if q > 0:
            ctx.wait("V", f"probs_used/{q - 1}")
        for tt in range(4):
            ctx.wait("V", f"lg/{q}")
            if tt > 0:
                ctx.wait("V", f"probs/{q}/{tt - 1}")
            vi = VV.reduce_max(rcol[:, tt:tt + 1],
                               ps_sc[:, b * 512 + tt * 32:b * 512 + (tt + 1) * 32],
                               axis=AX.X, negate=True)
            ctx.inc("V", vi, key=f"nm/{q}/{tt}")
            ctx.wait("S", f"nm/{q}/{tt}")
            si = SS.activation(probs[:, tt * 32:(tt + 1) * 32],
                               ps_sc[:, b * 512 + tt * 32:b * 512 + (tt + 1) * 32],
                               AF.Exp, bias=rcol[:, tt:tt + 1])
            ctx.inc("S", si, key=f"probs/{q}/{tt}")
        ctx.sc_release(b, f"probs/{q}/3")
        for tt in range(4):
            ctx.wait("V", f"probs/{q}/{tt}")
            pr = probs[:, tt * 32:(tt + 1) * 32]
            ctx.fence("V", VV.max(top8[:, tt * 8:(tt + 1) * 8], pr))
            ctx.fence("V", VV.tensor_reduce(rcol[:, 4 + tt:5 + tt],
                                            top8[:, tt * 8:(tt + 1) * 8],
                                            AX.X, ALU.add))
            ctx.fence("V", VV.reciprocal(rcol[:, 4 + tt:5 + tt],
                                         rcol[:, 4 + tt:5 + tt]))
            ctx.fence("V", VV.tensor_scalar(msk, pr,
                                            top8[:, tt * 8 + 7:tt * 8 + 8],
                                            None, ALU.is_ge))
            if q > 0:
                ctx.wait("V", f"wTm/{q - 1}")
            vi = VV.scalar_tensor_tensor(w_tm[:, tt * 32:(tt + 1) * 32],
                                         pr, rcol[:, 4 + tt:5 + tt], msk,
                                         ALU.mult, ALU.mult)
            ctx.inc("V", vi, key=f"w/{q}/{tt}")
        ctx.mark("V", f"probs_used/{q}")
        b = ctx.sc_acquire("T")
        ctx.wait("T", f"w/{q}/3")
        for tt in range(4):
            ti = TT.transpose(ps_sc[0:32, b * 512 + tt * 128:b * 512 + (tt + 1) * 128],
                              w_tm[:, tt * 32:(tt + 1) * 32], ident)
        ctx.inc("T", ti, key=f"wTm/{q}")
        ctx.wait("S", f"wTm/{q}")
        if q > 0:
            ctx.wait("S", f"wts/{q - 1}")
        si = SS.copy(wTq, ps_sc[0:32, b * 512:b * 512 + 512])
        ctx.inc("S", si, key=f"wT/{q}")
        ctx.sc_release(b, f"wT/{q}")
        if q > 0:
            ctx.wait("sync", f"wbcd/{q - 1}/31")
        ctx.wait("sync", f"wT/{q}")
        ctx.dma("sync", "SWT", wts_d, wTq, key=f"wts/{q}")

    def _w_fp32(q, e):
        return q == 0 or (q == 1 and e % 2 == 1)

    def _w_writes(q, e):
        return (q == 0 and e % 2 == 0) or (q == 1 and e % 2 == 1)

    def _wd_load(q, e, hc):
        qe = f"{q}/{e}"
        r = hc % 8
        if hc < 8:
            if e > 0:
                prev = (q, e - 1, r + 8)
            elif q > 0:
                prev = (q - 1, NE - 1, r + 8)
            else:
                prev = None
        else:
            prev = (q, e, r)
        fp32 = _w_fp32(q, e)
        weng = "gpsimd" if fp32 else "sync"
        if prev:
            ctx.wait(weng, f"dn/{prev[0]}/{prev[1]}/{prev[2]}")
            if _w_writes(prev[0], prev[1]):
                ctx.wait(weng, f"wrd/{prev[1]}/{prev[2]}")
        dst = wdR[:, r * 768:(r + 1) * 768].rearrange("p (i h) -> p i h", i=NI)
        if fp32:
            src = wd_d[e, :, hc * 128:(hc + 1) * 128].rearrange(
                "(i p) h -> p i h", p=128)
            ctx.dma("gpsimd", f"SDg{r}", dst, src, key=f"wdl/{qe}/{hc}")
        else:
            ctx.wait("sync", f"wrd/{e}/{hc}")
            ctx.dma("sync", f"SDs{r}", dst,
                    cwd_d[e, hc].rearrange("p (i h) -> p i h", i=NI),
                    key=f"wdl/{qe}/{hc}")

    def weight_loads_early(q, e):
        qe = f"{q}/{e}"
        if e > 0:
            pg = (q, e - 1)
        elif q > 0:
            pg = (q - 1, NE - 1)
        else:
            pg = None
        if pg:
            ctx.wait("gpsimd", f"ge/{pg[0]}/{pg[1]}/5")
            if _w_writes(pg[0], pg[1]):
                ctx.wait("gpsimd", f"wrg/{pg[1]}")
        if _w_fp32(q, e):
            ctx.dma("gpsimd", "SWg", wgS.rearrange("p (k i) -> p k i", k=NCK),
                    wg_d[e].rearrange("(k p) i -> p k i", p=128),
                    key=f"wgl/{qe}")
        else:
            ctx.wait("gpsimd", f"wrg/{e}")
            ctx.dma("gpsimd", "SWg", wgS, cwg_d[e], key=f"wgl/{qe}")
        if pg:
            ctx.wait("gpsimd", f"ue/{pg[0]}/{pg[1]}/5")
            if _w_writes(pg[0], pg[1]):
                ctx.wait("gpsimd", f"wru/{pg[1]}")
        if _w_fp32(q, e):
            ctx.dma("gpsimd", "SWu", wuS.rearrange("p (k i) -> p k i", k=NCK),
                    wu_d[e].rearrange("(k p) i -> p k i", p=128),
                    key=f"wul/{qe}")
        else:
            ctx.wait("gpsimd", f"wru/{e}")
            ctx.dma("gpsimd", "SWu", wuS, cwu_d[e], key=f"wul/{qe}")
        for hc in range(8):
            _wd_load(q, e, hc)

    def weight_loads_late(q, e):
        for hc in range(8, NCK):
            _wd_load(q, e, hc)

    def scratch_writes(q, e):
        ctx.wait("sync", f"wgl/{q}/{e}")
        ctx.dma("sync", "SR", cwg_d[e], wgS, key=f"wrg/{e}")
        ctx.wait("sync", f"wul/{q}/{e}")
        ctx.dma("sync", "SR", cwu_d[e], wuS, key=f"wru/{e}")
        for hc in range(NCK):
            r = hc % 8
            ctx.wait("sync", f"wdl/{q}/{e}/{hc}")
            ctx.dma("sync", "SR", cwd_d[e, hc], wdR[:, r * 768:(r + 1) * 768],
                    key=f"wrd/{e}/{hc}")

    def expert_slot(q, e):
        qe = f"{q}/{e}"
        par_h = e % 2
        if e >= 2:
            ctx.wait("sync", f"upw/{q}/{e - 2}/5")
        elif q > 0:
            ctx.wait("sync", f"upw/{q - 1}/{e + 30}/5")
        ctx.wait("sync", f"wts/{q}")
        ctx.dma("sync", "SWB", wbc[:, (e % 2) * TQ:(e % 2 + 1) * TQ].rearrange(
                    "p (n l) -> p n l", n=1),
                wts_d[e:e + 1, :].partition_broadcast(128), key=f"wbcd/{qe}")
        weight_loads_late(q, e)
        if _w_writes(q, e):
            scratch_writes(q, e)
        mb_bank = mamba_unit_pre(q, e // 16, e % 16)
        ctx.wait("T", f"wgl/{qe}")
        ctx.wait("T", f"xnbf/{q}/1/1/15")
        for ic in range(NI):
            p = ic % 2
            if ic >= 2:
                ctx.wait("T", f"silu/{qe}/{ic - 2}")
            elif e > 0:
                ctx.wait("T", f"silu/{q}/{e - 1}/{ic + 4}")
            elif q > 0:
                ctx.wait("T", f"silu/{q - 1}/31/{ic + 4}")
                ctx.wait("T", f"oadd/{q - 1}/3/{[0, 2][ic]}")
            for k in range(NCK):
                ti = TT.matmul(ps_g[p],
                               wgS[:, k * 768 + ic * 128:k * 768 + (ic + 1) * 128],
                               xnbf[:, k * TQ:(k + 1) * TQ],
                               start=(k == 0), stop=(k == NCK - 1))
            ctx.inc("T", ti, key=f"ge/{qe}/{ic}")
        ctx.wait("T", f"wul/{qe}")
        for ic in range(NI):
            p = ic % 2
            if ic >= 2:
                ctx.wait("T", f"upw/{qe}/{ic - 2}")
            elif e > 0:
                ctx.wait("T", f"upw/{q}/{e - 1}/{ic + 4}")
            elif q > 0:
                ctx.wait("T", f"upw/{q - 1}/31/{ic + 4}")
                ctx.wait("T", f"oadd/{q - 1}/3/{[1, 3][ic]}")
            for k in range(NCK):
                ti = TT.matmul(ps_u[p],
                               wuS[:, k * 768 + ic * 128:k * 768 + (ic + 1) * 128],
                               xnbf[:, k * TQ:(k + 1) * TQ],
                               start=(k == 0), stop=(k == NCK - 1))
            ctx.inc("T", ti, key=f"ue/{qe}/{ic}")
        if e == NE - 1:
            ctx.mark("T", f"xnbf_free_t/{q}")
        ctx.wait("T", f"hm/{qe}/5")
        for hc in range(NCK):
            p = hc % 2
            ctx.wait("T", f"wdl/{qe}/{hc}")
            if hc >= 2:
                ctx.wait("T", f"accv/{qe}/{hc - 2}")
            elif e > 0:
                ctx.wait("T", f"accv/{q}/{e - 1}/{hc + 14}")
            elif q > 0:
                ctx.wait("T", f"accv/{q - 1}/{NE - 1}/{hc + 14}")
            for ic in range(NI):
                ti = TT.matmul(ps_dn[:, p * 512:(p + 1) * 512],
                               wdR[:, (hc % 8) * 768 + ic * 128:(hc % 8) * 768 + (ic + 1) * 128],
                               h_e[:, par_h * NI * TQ + ic * TQ:par_h * NI * TQ + (ic + 1) * TQ],
                               start=(ic == 0), stop=(ic == NI - 1))
            ctx.inc("T", ti, key=f"dn/{qe}/{hc}")
        mb = mamba_steps(q, e // 16, e % 16, mb_bank)
        mb["sp"]()
        for n in range(8):
            mb["dA"](n)
        for ic in range(NI):
            p = ic % 2
            ps_ = ic % 4
            ctx.wait("S", f"ge/{qe}/{ic}")
            if ic >= 4:
                ctx.wait("S", f"hm/{qe}/{ic - 4}")
            elif e > 0:
                ctx.wait("S", f"hm/{q}/{e - 1}/{ic + 2}")
            elif q > 0:
                ctx.wait("S", f"hm/{q - 1}/{NE - 1}/{ic + 2}")
                ctx.wait("S", f"sgl/{q - 1}")
            si = SS.activation(sgb[:, ps_ * TQ:(ps_ + 1) * TQ], ps_g[p], AF.Silu)
            ctx.inc("S", si, key=f"silu/{qe}/{ic}")
        for n in range(8, NST):
            mb["dA"](n)
        mb["prep"]()
        for n in range(5):
            mb["step"](n)
        for ic in range(NI):
            p = ic % 2
            ctx.wait("V", f"ue/{qe}/{ic}")
            ctx.wait("V", f"wbcd/{qe}")
            vi = VV.tensor_tensor(upw[:, p * TQ:(p + 1) * TQ], ps_u[p],
                                  wbc[:, (e % 2) * TQ:(e % 2 + 1) * TQ], ALU.mult)
            ctx.inc("V", vi, key=f"upw/{qe}/{ic}")
            ctx.wait("V", f"silu/{qe}/{ic}")
            if e >= 2:
                ctx.wait("V", f"dn/{q}/{e - 2}/15")
            elif q > 0:
                ctx.wait("V", f"dn/{q - 1}/{e + 30}/15")
            ps_ = ic % 4
            vi = VV.tensor_tensor(
                h_e[:, par_h * NI * TQ + ic * TQ:par_h * NI * TQ + (ic + 1) * TQ],
                upw[:, p * TQ:(p + 1) * TQ], sgb[:, ps_ * TQ:(ps_ + 1) * TQ], ALU.mult)
            ctx.inc("V", vi, key=f"hm/{qe}/{ic}")
            mb["step"](5 + ic)
        nstep = 11
        for hc in range(NCK):
            p = hc % 2
            ctx.wait("V", f"dn/{qe}/{hc}")
            a_sl = acc[:, hc * TQ:(hc + 1) * TQ]
            if e == 0:
                if q > 0:
                    ctx.wait("V", f"csil_done/{q - 1}")
                vi = VV.tensor_copy(a_sl, ps_dn[:, p * 512:(p + 1) * 512])
            else:
                vi = VV.tensor_add(a_sl, ps_dn[:, p * 512:(p + 1) * 512], a_sl)
            ctx.inc("V", vi, key=f"accv/{qe}/{hc}")
            if hc % 3 == 2 and nstep < NST:
                mb["step"](nstep)
                nstep += 1
        while nstep < NST:
            mb["step"](nstep)
            nstep += 1
        mb["fin"]()
        if e + 1 < NE:
            weight_loads_early(q, e + 1)
        elif q + 1 < NQ:
            weight_loads_early(q + 1, 0)

    def mamba_unit_pre(q, ch, c):
        qc = f"{q}/{ch}/{c}"
        b = ctx.sc_acquire("T")
        ctx.wait("T", f"xproj/{q}/{ch}")
        ti = TT.matmul(ps_sc[:, b * 512:b * 512 + LCH],
                       WdtT[:, c * 128:(c + 1) * 128], xprojb,
                       start=True, stop=True)
        ctx.inc("T", ti, key=f"dtm/{qc}")
        return b

    def mamba_steps(q, ch, c, bank):
        u = ch * 16 + c
        qc = f"{q}/{ch}/{c}"

        def sp():
            ctx.wait("S", f"dtm/{qc}")
            ctx.wait("S", "prologue_s")
            if u >= 1 or q > 0:
                pu, pq = (u - 1, q) if u >= 1 else (31, q - 1)
                ctx.wait("S", f"dtf/{pq}/{pu // 16}/{pu % 16}")
            ps_z = ps_sc[:, bank * 512:bank * 512 + LCH]
            SS.activation(dtc, ps_z, AF.Relu, bias=bdt[:, c:c + 1])
            si = SS.activation(sp0, ps_z, AF.Abs, bias=bdt[:, c:c + 1])
            ctx.inc("S", si, key=f"dtps/{qc}")
            ctx.sc_release(bank, f"dtps/{qc}")
            SS.activation(sp1, sp0, AF.Exp, scale=-1.0)
            si = SS.activation(sp0, sp1, AF.Ln, bias=cpool[:, 1:2])
            ctx.inc("S", si, key=f"sp/{qc}")

        def dA(n):
            ctx.wait("S", f"dtf/{qc}")
            r = n % 8
            if n >= 8:
                ctx.wait("S", f"scan/{qc}/{n - 8}")
            else:
                pu, pq = (u - 1, q) if u >= 1 else ((31, q - 1) if q > 0 else (None, None))
                if pu is not None:
                    ctx.wait("S", f"scan/{pq}/{pu // 16}/{pu % 16}/{n + 8}")
            si = SS.activation(dAr[:, r * LCH:(r + 1) * LCH], dtc,
                               AF.Exp, scale=negP[:, c * NST + n:c * NST + n + 1])
            ctx.inc("S", si, key=f"dA/{qc}/{n}")

        def prep():
            ctx.wait("V", f"sp/{qc}")
            VV.tensor_add(dtc, dtc, sp0)
            vi = VV.tensor_tensor(dtxc, dtc,
                                  xnbf[:, c * TQ + ch * LCH:c * TQ + (ch + 1) * LCH],
                                  ALU.mult)
            ctx.inc("V", vi, key=f"dtf/{qc}")

        def step(n):
            r = n % 8
            ctx.wait("V", f"bc/{q}/{ch}")
            VV.tensor_tensor(dBr[:, (n % 2) * LCH:(n % 2 + 1) * LCH], dtxc,
                             Bbc[:, n * LCH:(n + 1) * LCH], ALU.mult)
            ctx.wait("V", f"dA/{qc}/{n}")
            vi = VV.tensor_tensor_scan(hsc[:, (n % 2) * LCH:(n % 2 + 1) * LCH],
                                       dAr[:, r * LCH:(r + 1) * LCH],
                                       dBr[:, (n % 2) * LCH:(n % 2 + 1) * LCH],
                                       hst[:, c * 16 + n:c * 16 + n + 1],
                                       ALU.mult, ALU.add)
            ctx.inc("V", vi, key=f"scan/{qc}/{n}")
            if n == 0:
                VV.tensor_tensor(y_c, hsc[:, 0:LCH], Cbc[:, 0:LCH], ALU.mult)
            else:
                VV.tensor_tensor(ytmp, hsc[:, (n % 2) * LCH:(n % 2 + 1) * LCH],
                                 Cbc[:, n * LCH:(n + 1) * LCH], ALU.mult)
                VV.tensor_add(y_c, ytmp, y_c)
            VV.tensor_copy(hst[:, c * 16 + n:c * 16 + n + 1],
                           hsc[:, (n % 2) * LCH + LCH - 1:(n % 2) * LCH + LCH])

        def fin():
            sl = u % 2
            if u >= 2:
                pu = u - 2
                ctx.wait("V", f"ssp/{q}/{pu // 16}/{pu % 16}")
            elif q > 0:
                pu = u + 30
                ctx.wait("V", f"ssp/{q - 1}/{pu // 16}/{pu % 16}")
            vi = VV.scalar_tensor_tensor(
                s_c[:, sl * LCH:(sl + 1) * LCH],
                xnbf[:, c * TQ + ch * LCH:c * TQ + (ch + 1) * LCH],
                Dv[:, c:c + 1], y_c, ALU.mult, ALU.add)
            ctx.inc("V", vi, key=f"sdone/{qc}")
            ctx.mark("V", f"unit/{q}/{ch}/{c}")
            ctx.wait("sync", f"sdone/{qc}")
            ctx.dma("sync", f"SS{c % 2}",
                    sfm_d[c * 128:(c + 1) * 128,
                          q * TQ + ch * LCH:q * TQ + (ch + 1) * LCH],
                    s_c[:, sl * LCH:(sl + 1) * LCH], key=f"ssp/{qc}")

        return {"sp": sp, "dA": dA, "prep": prep, "step": step, "fin": fin}

    def combine(q):
        ctx.wait("S", f"accv/{q}/{NE - 1}/15")
        for tt in range(4):
            tile = q * 4 + tt
            ch = tt // 2
            if tt > 0 or q > 0:
                pq, ptt = (q, tt - 1) if tt > 0 else (q - 1, 3)
                ctx.wait("sync", f"odma/{pq}/{ptt}")
            ctx.dma("sync", "SXM", x_tm, x_d[tile * 128:(tile + 1) * 128, :],
                    key=f"xtm/{q}/{tt}")
            ctx.wait("sync", f"ssp/{q}/{ch}/14")
            ctx.wait("sync", f"ssp/{q}/{ch}/15")
            if tt > 0:
                ctx.wait("sync", f"sg/{q}/{tt - 1}/15")
            elif q > 0:
                ctx.wait("sync", f"sg/{q - 1}/3/15")
            ctx.dma("sync", "SL0", s_sl.rearrange("p (c t) -> p c t", c=NCK),
                    sfm_d[:, q * TQ + tt * 128:q * TQ + (tt + 1) * 128].rearrange(
                        "(c p) t -> p c t", p=128), key=f"srl/{q}/{tt}")
            if tt > 0:
                ctx.wait("S", f"sg/{q}/{tt - 1}/15")
            acc3 = acc.rearrange("p (c t) -> p c t", c=NCK)[
                :, :, tt * 128:(tt + 1) * 128]
            si = SS.activation(sgb[:, 0:NCK * 128].rearrange(
                "p (c t) -> p c t", c=NCK), acc3, AF.Silu)
            ctx.inc("S", si, key=f"gsil/{q}/{tt}")
            for c in range(NCK):
                sl2 = c % 2
                ctx.wait("V", f"gsil/{q}/{tt}")
                ctx.wait("V", f"srl/{q}/{tt}")
                if c >= 2:
                    ctx.wait("V", f"ctp/{q}/{tt}/{c - 2}")
                elif tt > 0:
                    ctx.wait("V", f"ctp/{q}/{tt - 1}/{c + 14}")
                vi = VV.tensor_tensor(sg_c[:, sl2 * 128:(sl2 + 1) * 128],
                                      sgb[:, c * 128:(c + 1) * 128],
                                      s_sl[:, c * 128:(c + 1) * 128], ALU.mult)
                ctx.inc("V", vi, key=f"sg/{q}/{tt}/{c}")
                bank = c // 4
                ctx.wait("T", f"sg/{q}/{tt}/{c}")
                if tt == 0 and c == 0:
                    ctx.wait("T", f"accv/{q}/{NE - 1}/14")
                    ctx.wait("T", f"accv/{q}/{NE - 1}/15")
                    ctx.wait("T", f"silu/{q}/31/4")
                    ctx.wait("T", f"silu/{q}/31/5")
                    ctx.wait("T", f"upw/{q}/31/4")
                    ctx.wait("T", f"upw/{q}/31/5")
                if tt > 0:
                    ctx.wait("T", f"oadd/{q}/{tt - 1}/{bank}")
                ti = TT.transpose(ps_cb[bank][:, (c % 4) * 128:(c % 4 + 1) * 128],
                                  sg_c[:, sl2 * 128:(sl2 + 1) * 128], ident)
                ctx.inc("T", ti, key=f"ctp/{q}/{tt}/{c}")
            for g4 in range(4):
                ctx.wait("V", f"ctp/{q}/{tt}/{g4 * 4 + 3}")
                ctx.wait("V", f"xtm/{q}/{tt}")
                vi = VV.tensor_add(x_tm[:, g4 * 512:(g4 + 1) * 512], ps_cb[g4],
                                   x_tm[:, g4 * 512:(g4 + 1) * 512])
                ctx.inc("V", vi, key=f"oadd/{q}/{tt}/{g4}")
            ctx.wait("sync", f"oadd/{q}/{tt}/3")
            ctx.dma("sync", "SO", out_d[tile * 128:(tile + 1) * 128, :], x_tm,
                    key=f"odma/{q}/{tt}")
        ctx.mark("S", f"csil_done/{q}")
        ctx.mark("V", f"sgl/{q}")

    weight_loads_early(0, 0)
    for q in range(NQ):
        chunk_transposes(q, 0)
        chunk_transposes(q, 1)
        router(q)
        chunk_aux(q, 0)
        for e in range(NE):
            if e == 16:
                chunk_aux(q, 1)
            expert_slot(q, e)
        combine(q)
        if dbg and q == 0:
            ctx.wait("sync", f"xnbf/{q}/1/1/15")
            ctx.dma("sync", "SO", dbg_xnbf, xnbf, key="dbg_xnbf")
            ctx.wait("sync", f"accv/{q}/31/15")
            ctx.dma("sync", "SO", dbg_acc, acc, key="dbg_acc")
            ctx.wait("sync", f"w/{q}/3")
            ctx.dma("sync", "SO", dbg_w, w_tm, key="dbg_w")
    if dbg:
        ctx.wait("sync", f"sdone/{NQ - 1}/1/15")
        ctx.dma("sync", "SO", dbg_s, sfm_d, key="dbg_s")
    if ctx.real:
        for sm in ("SO", "SR", "SS0", "SS1", "SBS", "SWT"):
            if ctx.sems[sm].n:
                ctx.eng["sync"].wait_ge(ctx.sems[sm].h, ctx.sems[sm].n)


def build_program(dbg=False):
    ctx1 = Ctx(None)
    _build(ctx1, dbg=dbg)
    nc = bass.Bass("TRN2", target_bir_lowering=False, debug=False,
                   detect_race_conditions=False)
    ctx2 = Ctx(nc, tbl=ctx1.tbl)
    _build(ctx2, dbg=dbg)
    return nc


_CACHE = {}


def _in_maps(inputs):
    x = np.ascontiguousarray(inputs["L0_attn_x"], dtype=np.float32)
    shared = {
        "wx": np.ascontiguousarray(inputs["L0_attn_W_x"], dtype=np.float32),
        "wdt": np.ascontiguousarray(inputs["L0_attn_W_dt"], dtype=np.float32),
        "bdt_in": np.ascontiguousarray(
            inputs["L0_attn_b_dt"], dtype=np.float32).reshape(16, 128),
        "alog": np.ascontiguousarray(inputs["L0_attn_A_log"], dtype=np.float32),
        "dvec_in": np.ascontiguousarray(
            inputs["L0_attn_D"], dtype=np.float32).reshape(16, 128),
        "rw": np.ascontiguousarray(inputs["L0_ffn_router_w"], dtype=np.float32),
        "wgt": np.ascontiguousarray(inputs["L0_ffn_w_gate"], dtype=np.float32),
        "wup": np.ascontiguousarray(inputs["L0_ffn_w_up"], dtype=np.float32),
        "wdn": np.ascontiguousarray(inputs["L0_ffn_w_down"], dtype=np.float32),
        "ident": np.eye(128, dtype=np.float32),
    }
    return [{"x": x[i], **shared} for i in range(8)]


def kernel(**inputs):
    if "nc" not in _CACHE:
        _CACHE["nc"] = build_program()
    res = run_bass_kernel_spmd(_CACHE["nc"], _in_maps(inputs),
                               core_ids=list(range(8)))
    return np.stack([res.results[i]["out"] for i in range(8)], axis=0)


# version 09 (final)
# speedup vs baseline: 1.0575x; 1.0575x over previous
import numpy as np
import concourse.bass as bass
import concourse.mybir as mybir
from concourse.bass_utils import run_bass_kernel_spmd

F32 = mybir.dt.float32
BF16 = mybir.dt.bfloat16
AF = mybir.ActivationFunctionType
ALU = mybir.AluOpType
AX = mybir.AxisListType

D = 2048
T = 2048
NCK = 16
NQ = 4
TQ = 512
LCH = 256
NE = 32
NI = 6
NST = 16
NR = 64


class _Dummy:
    def __getattr__(self, k):
        return self

    def __call__(self, *a, **k):
        return self

    def __getitem__(self, k):
        return self


class Sem:
    def __init__(self, ctx, name):
        self.name = name
        self.h = ctx.nc.alloc_semaphore(name=name) if ctx.real else _Dummy()
        self.n = 0


class Ctx:
    def __init__(self, nc, tbl=None):
        self.real = nc is not None
        self.nc = nc if self.real else _Dummy()
        self.tbl = {} if tbl is None else tbl
        self.fill = tbl is None
        names = ["T", "V", "S", "SP", "SXx", "SXM", "SWB", "SBC", "SBS",
                 "SWT", "SO", "SR", "SS0", "SS1", "SWg", "SWu"]
        names += [f"SDg{r}" for r in range(8)] + [f"SDs{r}" for r in range(8)]
        names += [f"SL{i}" for i in range(4)]
        self.sems = {k: Sem(self, k) for k in names}
        self.wm = {}
        self.eng = {"T": self.nc.tensor, "V": self.nc.vector,
                    "S": self.nc.scalar, "sync": self.nc.sync,
                    "gpsimd": self.nc.gpsimd}
        for k in self.sems:
            if k not in ("T", "V", "S"):
                self.eng.setdefault(k, self.nc.sync)
        self.scb = [None, None]
        self.sc_i = 0

    def inc(self, semname, instr, k=1, key=None):
        s = self.sems[semname]
        if self.real:
            instr.then_inc(s.h, k)
        s.n += k
        if key is not None:
            self.mark(semname, key)
        return s.n

    def mark(self, semname, key):
        if self.fill:
            self.tbl[key] = (semname, self.sems[semname].n)
        else:
            assert self.tbl.get(key) == (semname, self.sems[semname].n), (
                f"pass mismatch at {key}: pass1={self.tbl.get(key)} "
                f"pass2=({semname}, {self.sems[semname].n})")

    def wait(self, engine, key):
        if self.fill:
            return
        if key not in self.tbl:
            raise KeyError(f"missing milestone {key}")
        semname, v = self.tbl[key]
        if v <= 0:
            return
        eng = self.eng[engine]
        wmk = (id(eng), semname)
        if self.wm.get(wmk, -1) >= v:
            return
        self.wm[wmk] = v
        if self.real:
            eng.wait_ge(self.sems[semname].h, v)

    def fence(self, semname, ins):
        n = self.inc(semname, ins)
        if self.real:
            self.eng[semname].wait_ge(self.sems[semname].h, n)

    def dma(self, qeng, sem, out, in_, key=None, **kw):
        eng = self.eng[qeng]
        sm = self.sems[sem]
        if sm.n > 0 and self.real:
            eng.wait_ge(sm.h, sm.n)
        ins = eng.dma_start(out=out, in_=in_, **kw)
        self.inc(sem, ins, 16, key=key)

    def sc_acquire(self, producer_eng):
        i = self.sc_i % 2
        self.sc_i += 1
        if self.scb[i] is not None:
            self.wait(producer_eng, self.scb[i])
        return i

    def sc_release(self, i, consumer_key):
        self.scb[i] = consumer_key


def _build(ctx, dbg=False):
    nc = ctx.nc
    x_d = nc.dram_tensor("x", [T, D], F32, kind="ExternalInput").ap()
    wx_d = nc.dram_tensor("wx", [96, D], F32, kind="ExternalInput").ap()
    wdt_d = nc.dram_tensor("wdt", [D, NR], F32, kind="ExternalInput").ap()
    bdt_d = nc.dram_tensor("bdt_in", [16, 128], F32, kind="ExternalInput").ap()
    alog_d = nc.dram_tensor("alog", [D, NST], F32, kind="ExternalInput").ap()
    dvec_d = nc.dram_tensor("dvec_in", [16, 128], F32, kind="ExternalInput").ap()
    rw_d = nc.dram_tensor("rw", [NE, D], F32, kind="ExternalInput").ap()
    wg_d = nc.dram_tensor("wgt", [NE, D, 768], F32, kind="ExternalInput").ap()
    wu_d = nc.dram_tensor("wup", [NE, D, 768], F32, kind="ExternalInput").ap()
    wd_d = nc.dram_tensor("wdn", [NE, 768, D], F32, kind="ExternalInput").ap()
    id_d = nc.dram_tensor("ident", [128, 128], F32, kind="ExternalInput").ap()
    out_d = nc.dram_tensor("out", [T, D], F32, kind="ExternalOutput").ap()
    sfm_d = nc.dram_tensor("sfm", [D, T], BF16, kind="Internal").ap()
    cwg_d = nc.dram_tensor("cwg", [NE, 128, NCK * 768], BF16, kind="Internal").ap()
    cwu_d = nc.dram_tensor("cwu", [NE, 128, NCK * 768], BF16, kind="Internal").ap()
    cwd_d = nc.dram_tensor("cwd", [NE, NCK, 128, NI * 128], BF16, kind="Internal").ap()
    bcs_d = nc.dram_tensor("bcs", [32, LCH], BF16, kind="Internal").ap()
    wts_d = nc.dram_tensor("wts", [NE, TQ], BF16, kind="Internal").ap()
    if dbg:
        dbg_xt = nc.dram_tensor("dbg_xt", [128, 2 * D], F32, kind="ExternalOutput").ap()
        dbg_mb = nc.dram_tensor("dbg_mb", [128, 8 * LCH], F32, kind="ExternalOutput").ap()
        dbg_bc = nc.dram_tensor("dbg_bc", [128, 2 * NST * LCH], BF16, kind="ExternalOutput").ap()
        dbg_xp = nc.dram_tensor("dbg_xp", [96, LCH], F32, kind="ExternalOutput").ap()
        dbg_xnbf = nc.dram_tensor("dbg_xnbf", [128, NCK * TQ], BF16, kind="ExternalOutput").ap()
        dbg_s = nc.dram_tensor("dbg_s", [D, T], F32, kind="ExternalOutput").ap()
        dbg_acc = nc.dram_tensor("dbg_acc", [128, NCK * TQ], F32, kind="ExternalOutput").ap()
        dbg_w = nc.dram_tensor("dbg_w", [128, 4 * 32], F32, kind="ExternalOutput").ap()

    A = lambda n, shp, dt: nc.alloc_sbuf_tensor(n, shp, dt).ap()
    ident = A("identS", [128, 128], F32)
    cpool = A("cpool", [128, 4], F32)
    WxT = A("WxT", [128, NCK * 96], BF16)
    WdtT = A("WdtT", [64, NCK * 128], BF16)
    rwT = A("rwT", [128, NCK * 32], BF16)
    negP = A("negP", [128, NCK * NST], F32)
    bdt = A("bdtS", [128, 16], F32)
    Dv = A("DvS", [128, 16], F32)
    x_t = A("x_t", [128, D], F32)
    xnbf = A("xnbf", [128, NCK * TQ], BF16)
    acc = A("acc", [128, NCK * TQ], F32)
    wgS = A("wgS", [128, NCK * 768], BF16)
    wuS = A("wuS", [128, NCK * 768], BF16)
    wdR = A("wdR", [128, 8 * 768], BF16)
    h_e = A("h_e", [128, 2 * NI * TQ], BF16)
    sgb = A("sgb", [128, 4 * TQ], BF16)
    upw = A("upw", [128, 2 * TQ], F32)
    wbc = A("wbc", [128, 2 * TQ], BF16)
    Bbc = A("Bbc", [128, NST * LCH], BF16)
    Cbc = A("Cbc", [128, NST * LCH], BF16)
    dtc = A("dtc", [128, LCH], F32)
    dtxc = A("dtxc", [128, LCH], F32)
    sp0 = A("sp0", [128, LCH], F32)
    sp1 = A("sp1", [128, LCH], F32)
    dAr = A("dAr", [128, 8 * LCH], F32)
    dBr = A("dBr", [128, 2 * LCH], F32)
    hsc = A("hsc", [128, 2 * LCH], F32)
    ytmp = A("ytmp", [128, LCH], F32)
    y_c = A("y_c", [128, LCH], F32)
    hst = A("hst", [128, 256], F32)
    s_c = A("s_c", [128, 2 * LCH], BF16)
    xproj = A("xproj", [96, LCH], F32)
    xprojbc = A("xprojbc", [96, LCH], BF16)
    xprojb = A("xprojb", [64, LCH], BF16)
    probs = A("probs", [128, 4 * 32], F32)
    top8 = A("top8", [128, 4 * 8], F32)
    rcol = A("rcol", [128, 12], F32)
    msk = A("msk", [128, 32], F32)
    w_tm = A("w_tm", [128, 4 * 32], F32)
    wTq = A("wTq", [32, TQ], BF16)
    s_sl = A("s_sl", [128, NCK * 128], BF16)
    sg_c = A("sg_c", [128, 2 * 128], F32)
    x_tm = A("x_tm", [128, D], F32)
    stat = A("stat", [128, 12], F32)

    P = lambda n, shp: nc.alloc_psum_tensor(n, shp, F32).ap()
    ps_g = [P("ps_gA", [128, 512]), P("ps_gB", [128, 512])]
    ps_u = [P("ps_uA", [128, 512]), P("ps_uB", [128, 512])]
    ps_dn = P("ps_dn", [128, 1024])
    ps_sc = P("ps_sc", [128, 1024])
    ps_cb = [ps_g[0], ps_u[0], ps_g[1], ps_u[1]]

    TT, VV, SS = ctx.eng["T"], ctx.eng["V"], ctx.eng["S"]

    ctx.dma("sync", "SP", ident, id_d, key="ld_id")
    ctx.dma("sync", "SP", x_t[0:96, 0:D], wx_d, key="ld_wx")
    ctx.dma("sync", "SP", negP.rearrange("p (c n) -> p c n", c=NCK),
            alog_d.rearrange("(c p) n -> p c n", p=128),
            key="ld_alog", allow_slow_non_contiguous=True)
    VV.memset(cpool[:, 0:1], 1e-5)
    VV.memset(cpool[:, 1:2], 1.0)
    VV.memset(cpool[:, 2:3], 0.125)
    VV.memset(cpool[:, 3:4], 1.0 / D)
    vi = VV.memset(hst, 0.0)
    ctx.inc("V", vi, key="consts")
    ctx.wait("S", "ld_alog")
    si = SS.activation(negP, negP, AF.Exp)
    ctx.inc("S", si, key="neg_exp")
    ctx.wait("V", "neg_exp")
    vi = VV.tensor_scalar_mul(negP, negP, -1.0)
    ctx.inc("V", vi, key="negP")

    ctx.wait("T", "ld_id")
    ctx.wait("T", "ld_wx")
    for k in range(NCK):
        b = ctx.sc_acquire("T")
        ti = TT.transpose(ps_sc[:, b * 512:b * 512 + 96],
                          x_t[0:96, k * 128:(k + 1) * 128], ident[0:96, 0:96])
        ctx.inc("T", ti, key=f"pt_wx{k}")
        ctx.wait("S", f"pt_wx{k}")
        si = SS.copy(WxT[:, k * 96:(k + 1) * 96], ps_sc[:, b * 512:b * 512 + 96])
        ctx.inc("S", si, key=f"pe_wx{k}")
        ctx.sc_release(b, f"pe_wx{k}")
    ctx.wait("SP", "pt_wx15")
    ctx.dma("sync", "SP", x_t[:, 0:NCK * NR].rearrange("p (c r) -> p c r", c=NCK),
            wdt_d.rearrange("(c p) r -> p c r", p=128), key="ld_wdt")
    ctx.dma("sync", "SP", x_t[0:16, 1024:1024 + 128], bdt_d, key="ld_bdt")
    ctx.dma("sync", "SP", x_t[0:16, 1152:1152 + 128], dvec_d, key="ld_dv")
    ctx.wait("T", "ld_wdt")
    ctx.wait("T", "ld_bdt")
    ctx.wait("T", "ld_dv")
    for c in range(NCK):
        b = ctx.sc_acquire("T")
        ti = TT.transpose(ps_sc[0:64, b * 512:b * 512 + 128],
                          x_t[:, c * 64:(c + 1) * 64], ident)
        ctx.inc("T", ti, key=f"pt_wdt{c}")
        ctx.wait("S", f"pt_wdt{c}")
        si = SS.copy(WdtT[:, c * 128:(c + 1) * 128],
                     ps_sc[0:64, b * 512:b * 512 + 128])
        ctx.inc("S", si, key=f"pe_wdt{c}")
        ctx.sc_release(b, f"pe_wdt{c}")
    b = ctx.sc_acquire("T")
    ti = TT.transpose(ps_sc[0:128, b * 512:b * 512 + 16],
                      x_t[0:16, 1024:1024 + 128], ident[0:16, 0:16])
    ti = TT.transpose(ps_sc[0:128, b * 512 + 16:b * 512 + 32],
                      x_t[0:16, 1152:1152 + 128], ident[0:16, 0:16])
    ctx.inc("T", ti, key="pt_bd")
    ctx.wait("S", "pt_bd")
    si = SS.copy(bdt, ps_sc[:, b * 512:b * 512 + 16])
    si = SS.copy(Dv, ps_sc[:, b * 512 + 16:b * 512 + 32])
    ctx.inc("S", si, key="pe_bd")
    ctx.sc_release(b, "pe_bd")
    ctx.wait("SP", "pt_bd")
    ctx.dma("sync", "SP", x_t[0:32, 0:D], rw_d, key="ld_rw")
    ctx.wait("T", "ld_rw")
    for k in range(NCK):
        b = ctx.sc_acquire("T")
        ti = TT.transpose(ps_sc[:, b * 512:b * 512 + 32],
                          x_t[0:32, k * 128:(k + 1) * 128], ident[0:32, 0:32])
        ctx.inc("T", ti, key=f"pt_rw{k}")
        ctx.wait("S", f"pt_rw{k}")
        si = SS.copy(rwT[:, k * 32:(k + 1) * 32], ps_sc[:, b * 512:b * 512 + 32])
        ctx.inc("S", si, key=f"pe_rw{k}")
        ctx.sc_release(b, f"pe_rw{k}")
    ctx.mark("S", "prologue_s")
    ctx.mark("T", "prologue_t")

    def chunk_transposes(q, ch):
        if q > 0:
            ctx.wait("S", f"xnbf_free_t/{q - 1}")
            ctx.wait("S", f"sdone/{q - 1}/1/15")
        for j in range(2):
            tile = q * 4 + ch * 2 + j
            prev = (q, ch, j - 1) if j > 0 else (
                (q, ch - 1, 1) if ch > 0 else ((q - 1, 1, 1) if q > 0 else None))
            if prev is not None:
                ctx.wait("sync", f"tp/{prev[0]}/{prev[1]}/{prev[2]}/15")
            else:
                ctx.wait("sync", "pt_rw15")
            ctx.dma("sync", "SXx", x_t,
                    x_d[tile * 128:(tile + 1) * 128, :], key=f"xl/{q}/{ch}/{j}")
            ctx.wait("S", f"xl/{q}/{ch}/{j}")
            if q > 0 and (ch, j) == (0, 0):
                ctx.wait("S", f"scan/{q - 1}/1/15/15")
            xt = x_t
            SS.activation(dAr, xt, AF.Square, accum_out=stat[:, 0:1])
            si = SS.activation(dAr, xt, AF.Copy, accum_out=stat[:, 1:2])
            ctx.inc("S", si, key=f"lns/{q}/{ch}/{j}")
            ctx.wait("V", f"lns/{q}/{ch}/{j}")
            ctx.wait("V", "consts")
            ctx.fence("V", VV.tensor_mul(stat[:, 2:3], stat[:, 1:2], cpool[:, 3:4]))
            ctx.fence("V", VV.tensor_mul(stat[:, 3:4], stat[:, 2:3], stat[:, 2:3]))
            ctx.fence("V", VV.tensor_mul(stat[:, 4:5], stat[:, 0:1], cpool[:, 3:4]))
            vi = VV.tensor_sub(stat[:, 5:6], stat[:, 4:5], stat[:, 3:4])
            ctx.inc("V", vi, key=f"var/{q}/{ch}/{j}")
            ctx.wait("S", f"var/{q}/{ch}/{j}")
            si = SS.activation(stat[:, 6:7], stat[:, 5:6], AF.Sqrt,
                               bias=cpool[:, 0:1])
            ctx.inc("S", si, key=f"sqv/{q}/{ch}/{j}")
            ctx.wait("V", f"sqv/{q}/{ch}/{j}")
            ctx.fence("V", VV.reciprocal(stat[:, 6:7], stat[:, 6:7]))
            vi = VV.tensor_scalar(xt, xt, stat[:, 2:3], stat[:, 6:7],
                                  ALU.subtract, ALU.mult)
            ctx.inc("V", vi, key=f"ln/{q}/{ch}/{j}")
            for c in range(NCK):
                bk = ctx.sc_acquire("T")
                ctx.wait("T", f"ln/{q}/{ch}/{j}")
                ti = TT.transpose(ps_sc[:, bk * 512:bk * 512 + 128],
                                  x_t[:, c * 128:(c + 1) * 128], ident)
                ctx.inc("T", ti, key=f"tp/{q}/{ch}/{j}/{c}")
                ctx.wait("S", f"tp/{q}/{ch}/{j}/{c}")
                col = c * TQ + ch * LCH + j * 128
                si = SS.copy(xnbf[:, col:col + 128],
                             ps_sc[:, bk * 512:bk * 512 + 128])
                ctx.inc("S", si, key=f"xnbf/{q}/{ch}/{j}/{c}")
                ctx.sc_release(bk, f"xnbf/{q}/{ch}/{j}/{c}")

    def chunk_aux(q, ch):
        b = ctx.sc_acquire("T")
        ctx.wait("T", f"xnbf/{q}/{ch}/1/15")
        for k in range(NCK):
            ti = TT.matmul(ps_sc[0:96, b * 512:b * 512 + LCH],
                           WxT[:, k * 96:(k + 1) * 96],
                           xnbf[:, k * TQ + ch * LCH:k * TQ + (ch + 1) * LCH],
                           start=(k == 0), stop=(k == NCK - 1))
        ctx.inc("T", ti, key=f"xpm/{q}/{ch}")
        ctx.wait("S", f"xpm/{q}/{ch}")
        if ch == 1:
            ctx.wait("S", f"dtm/{q}/0/15")
        elif q > 0:
            ctx.wait("S", f"dtm/{q - 1}/1/15")
        SS.copy(xproj, ps_sc[0:96, b * 512:b * 512 + LCH])
        SS.copy(xprojb, ps_sc[0:64, b * 512:b * 512 + LCH])
        si = SS.copy(xprojbc[64:96, :], ps_sc[64:96, b * 512:b * 512 + LCH])
        ctx.inc("S", si, key=f"xproj/{q}/{ch}")
        ctx.sc_release(b, f"xproj/{q}/{ch}")
        if ch == 1:
            ctx.wait("sync", f"bc/{q}/0")
        elif q > 0:
            ctx.wait("sync", f"bc/{q - 1}/1")
        ctx.wait("sync", f"xproj/{q}/{ch}")
        ctx.dma("sync", "SBS", bcs_d, xprojbc[64:96, :], key=f"bcs/{q}/{ch}")
        if ch == 0 and q > 0:
            ctx.wait("sync", f"unit/{q - 1}/1/15")
        if ch == 1:
            ctx.wait("sync", f"unit/{q}/0/15")
        ctx.wait("sync", f"bcs/{q}/{ch}")
        ctx.dma("sync", "SBC", Bbc.rearrange("p (n l) -> p n l", n=NST),
                bcs_d[0:16, :].partition_broadcast(128))
        ctx.dma("sync", "SBC", Cbc.rearrange("p (n l) -> p n l", n=NST),
                bcs_d[16:32, :].partition_broadcast(128), key=f"bc/{q}/{ch}")

    def router(q):
        b = ctx.sc_acquire("T")
        ctx.wait("T", f"xnbf/{q}/1/1/15")
        ctx.wait("T", "prologue_s")
        for tt in range(4):
            for k in range(NCK):
                ti = TT.matmul(
                    ps_sc[:, b * 512 + tt * 32:b * 512 + (tt + 1) * 32],
                    xnbf[:, k * TQ + tt * 128:k * TQ + (tt + 1) * 128],
                    rwT[:, k * 32:(k + 1) * 32],
                    start=(k == 0), stop=(k == NCK - 1))
        ctx.inc("T", ti, key=f"lg/{q}")
        if q > 0:
            ctx.wait("V", f"probs_used/{q - 1}")
        for tt in range(4):
            ctx.wait("V", f"lg/{q}")
            if tt > 0:
                ctx.wait("V", f"probs/{q}/{tt - 1}")
            vi = VV.reduce_max(rcol[:, tt:tt + 1],
                               ps_sc[:, b * 512 + tt * 32:b * 512 + (tt + 1) * 32],
                               axis=AX.X, negate=True)
            ctx.inc("V", vi, key=f"nm/{q}/{tt}")
            ctx.wait("S", f"nm/{q}/{tt}")
            si = SS.activation(probs[:, tt * 32:(tt + 1) * 32],
                               ps_sc[:, b * 512 + tt * 32:b * 512 + (tt + 1) * 32],
                               AF.Exp, bias=rcol[:, tt:tt + 1])
            ctx.inc("S", si, key=f"probs/{q}/{tt}")
        ctx.sc_release(b, f"probs/{q}/3")
        for tt in range(4):
            ctx.wait("V", f"probs/{q}/{tt}")
            pr = probs[:, tt * 32:(tt + 1) * 32]
            ctx.fence("V", VV.max(top8[:, tt * 8:(tt + 1) * 8], pr))
            ctx.fence("V", VV.tensor_reduce(rcol[:, 4 + tt:5 + tt],
                                            top8[:, tt * 8:(tt + 1) * 8],
                                            AX.X, ALU.add))
            ctx.fence("V", VV.reciprocal(rcol[:, 4 + tt:5 + tt],
                                         rcol[:, 4 + tt:5 + tt]))
            ctx.fence("V", VV.tensor_scalar(msk, pr,
                                            top8[:, tt * 8 + 7:tt * 8 + 8],
                                            None, ALU.is_ge))
            if q > 0:
                ctx.wait("V", f"wTm/{q - 1}")
            vi = VV.scalar_tensor_tensor(w_tm[:, tt * 32:(tt + 1) * 32],
                                         pr, rcol[:, 4 + tt:5 + tt], msk,
                                         ALU.mult, ALU.mult)
            ctx.inc("V", vi, key=f"w/{q}/{tt}")
        ctx.mark("V", f"probs_used/{q}")
        b = ctx.sc_acquire("T")
        ctx.wait("T", f"w/{q}/3")
        for tt in range(4):
            ti = TT.transpose(ps_sc[0:32, b * 512 + tt * 128:b * 512 + (tt + 1) * 128],
                              w_tm[:, tt * 32:(tt + 1) * 32], ident)
        ctx.inc("T", ti, key=f"wTm/{q}")
        ctx.wait("S", f"wTm/{q}")
        if q > 0:
            ctx.wait("S", f"wts/{q - 1}")
        si = SS.copy(wTq, ps_sc[0:32, b * 512:b * 512 + 512])
        ctx.inc("S", si, key=f"wT/{q}")
        ctx.sc_release(b, f"wT/{q}")
        if q > 0:
            ctx.wait("sync", f"wbcd/{q - 1}/31")
        ctx.wait("sync", f"wT/{q}")
        ctx.dma("sync", "SWT", wts_d, wTq, key=f"wts/{q}")

    def _w_fp32(q, e):
        return q == 0 or (q == 1 and e % 2 == 1)

    def _w_writes(q, e):
        return (q == 0 and e % 2 == 0) or (q == 1 and e % 2 == 1)

    def _wd_load(q, e, hc):
        qe = f"{q}/{e}"
        r = hc % 8
        if hc < 8:
            if e > 0:
                prev = (q, e - 1, r + 8)
            elif q > 0:
                prev = (q - 1, NE - 1, r + 8)
            else:
                prev = None
        else:
            prev = (q, e, r)
        fp32 = _w_fp32(q, e)
        weng = "gpsimd" if fp32 else "sync"
        if prev:
            ctx.wait(weng, f"dn/{prev[0]}/{prev[1]}/{prev[2]}")
            if _w_writes(prev[0], prev[1]):
                ctx.wait(weng, f"wrd/{prev[1]}/{prev[2]}")
        dst = wdR[:, r * 768:(r + 1) * 768].rearrange("p (i h) -> p i h", i=NI)
        if fp32:
            src = wd_d[e, :, hc * 128:(hc + 1) * 128].rearrange(
                "(i p) h -> p i h", p=128)
            ctx.dma("gpsimd", f"SDg{r}", dst, src, key=f"wdl/{qe}/{hc}")
        else:
            ctx.wait("sync", f"wrd/{e}/{hc}")
            ctx.dma("sync", f"SDs{r}", dst,
                    cwd_d[e, hc].rearrange("p (i h) -> p i h", i=NI),
                    key=f"wdl/{qe}/{hc}")

    def weight_loads_early(q, e):
        qe = f"{q}/{e}"
        if e > 0:
            pg = (q, e - 1)
        elif q > 0:
            pg = (q - 1, NE - 1)
        else:
            pg = None
        if pg:
            ctx.wait("gpsimd", f"ge/{pg[0]}/{pg[1]}/5")
            if _w_writes(pg[0], pg[1]):
                ctx.wait("gpsimd", f"wrg/{pg[1]}")
        if _w_fp32(q, e):
            ctx.dma("gpsimd", "SWg", wgS.rearrange("p (k i) -> p k i", k=NCK),
                    wg_d[e].rearrange("(k p) i -> p k i", p=128),
                    key=f"wgl/{qe}")
        else:
            ctx.wait("gpsimd", f"wrg/{e}")
            ctx.dma("gpsimd", "SWg", wgS, cwg_d[e], key=f"wgl/{qe}")
        if pg:
            ctx.wait("gpsimd", f"ue/{pg[0]}/{pg[1]}/5")
            if _w_writes(pg[0], pg[1]):
                ctx.wait("gpsimd", f"wru/{pg[1]}")
        if _w_fp32(q, e):
            ctx.dma("gpsimd", "SWu", wuS.rearrange("p (k i) -> p k i", k=NCK),
                    wu_d[e].rearrange("(k p) i -> p k i", p=128),
                    key=f"wul/{qe}")
        else:
            ctx.wait("gpsimd", f"wru/{e}")
            ctx.dma("gpsimd", "SWu", wuS, cwu_d[e], key=f"wul/{qe}")
        for hc in range(8):
            _wd_load(q, e, hc)

    def weight_loads_late(q, e):
        for hc in range(8, NCK):
            _wd_load(q, e, hc)

    def scratch_writes(q, e):
        ctx.wait("sync", f"wgl/{q}/{e}")
        ctx.dma("sync", "SR", cwg_d[e], wgS, key=f"wrg/{e}")
        ctx.wait("sync", f"wul/{q}/{e}")
        ctx.dma("sync", "SR", cwu_d[e], wuS, key=f"wru/{e}")
        for hc in range(NCK):
            r = hc % 8
            ctx.wait("sync", f"wdl/{q}/{e}/{hc}")
            ctx.dma("sync", "SR", cwd_d[e, hc], wdR[:, r * 768:(r + 1) * 768],
                    key=f"wrd/{e}/{hc}")

    def expert_slot(q, e):
        qe = f"{q}/{e}"
        par_h = e % 2
        if e >= 2:
            ctx.wait("sync", f"upw/{q}/{e - 2}/5")
        elif q > 0:
            ctx.wait("sync", f"upw/{q - 1}/{e + 30}/5")
        ctx.wait("sync", f"wts/{q}")
        ctx.dma("sync", "SWB", wbc[:, (e % 2) * TQ:(e % 2 + 1) * TQ].rearrange(
                    "p (n l) -> p n l", n=1),
                wts_d[e:e + 1, :].partition_broadcast(128), key=f"wbcd/{qe}")
        weight_loads_late(q, e)
        if _w_writes(q, e):
            scratch_writes(q, e)
        mb_bank = mamba_unit_pre(q, e // 16, e % 16)
        ctx.wait("T", f"wgl/{qe}")
        ctx.wait("T", f"xnbf/{q}/1/1/15")
        for ic in range(NI):
            p = ic % 2
            if ic >= 2:
                ctx.wait("T", f"silu/{qe}/{ic - 2}")
            elif e > 0:
                ctx.wait("T", f"silu/{q}/{e - 1}/{ic + 4}")
                if ic == 0:
                    ctx.wait("T", f"accv/{q}/{e - 1}/14")
            elif q > 0:
                ctx.wait("T", f"silu/{q - 1}/31/{ic + 4}")
                ctx.wait("T", f"oadd/{q - 1}/3/{[0, 2][ic]}")
                if ic == 0:
                    ctx.wait("T", f"accv/{q - 1}/{NE - 1}/14")
            for k in range(NCK):
                ti = TT.matmul(ps_g[p],
                               wgS[:, k * 768 + ic * 128:k * 768 + (ic + 1) * 128],
                               xnbf[:, k * TQ:(k + 1) * TQ],
                               start=(k == 0), stop=(k == NCK - 1))
            ctx.inc("T", ti, key=f"ge/{qe}/{ic}")
        ctx.wait("T", f"wul/{qe}")
        for ic in range(NI):
            p = ic % 2
            if ic >= 2:
                ctx.wait("T", f"upw/{qe}/{ic - 2}")
            elif e > 0:
                ctx.wait("T", f"upw/{q}/{e - 1}/{ic + 4}")
                if ic == 0:
                    ctx.wait("T", f"accv/{q}/{e - 1}/15")
            elif q > 0:
                ctx.wait("T", f"upw/{q - 1}/31/{ic + 4}")
                ctx.wait("T", f"oadd/{q - 1}/3/{[1, 3][ic]}")
                if ic == 0:
                    ctx.wait("T", f"accv/{q - 1}/{NE - 1}/15")
            for k in range(NCK):
                ti = TT.matmul(ps_u[p],
                               wuS[:, k * 768 + ic * 128:k * 768 + (ic + 1) * 128],
                               xnbf[:, k * TQ:(k + 1) * TQ],
                               start=(k == 0), stop=(k == NCK - 1))
            ctx.inc("T", ti, key=f"ue/{qe}/{ic}")
        if e == NE - 1:
            ctx.mark("T", f"xnbf_free_t/{q}")
        ctx.wait("T", f"hm/{qe}/5")
        def dn_bank(hc):
            r4 = hc % 4
            if r4 == 0:
                return ps_dn[:, 0:512]
            if r4 == 1:
                return ps_dn[:, 512:1024]
            return ps_g[0] if r4 == 2 else ps_u[0]
        for hc in range(NCK):
            ctx.wait("T", f"wdl/{qe}/{hc}")
            if hc >= 4:
                ctx.wait("T", f"accv/{qe}/{hc - 4}")
            elif e > 0:
                ctx.wait("T", f"accv/{q}/{e - 1}/{hc + 12}")
            elif q > 0:
                ctx.wait("T", f"accv/{q - 1}/{NE - 1}/{hc + 12}")
            if hc % 4 == 2:
                ctx.wait("T", f"silu/{qe}/4")
            elif hc % 4 == 3:
                ctx.wait("T", f"upw/{qe}/4")
            tgt = dn_bank(hc)
            for ic in range(NI):
                ti = TT.matmul(tgt,
                               wdR[:, (hc % 8) * 768 + ic * 128:(hc % 8) * 768 + (ic + 1) * 128],
                               h_e[:, par_h * NI * TQ + ic * TQ:par_h * NI * TQ + (ic + 1) * TQ],
                               start=(ic == 0), stop=(ic == NI - 1))
            ctx.inc("T", ti, key=f"dn/{qe}/{hc}")
        mb = mamba_steps(q, e // 16, e % 16, mb_bank)
        mb["sp"]()
        for n in range(8):
            mb["dA"](n)
        for ic in range(NI):
            p = ic % 2
            ps_ = ic % 4
            ctx.wait("S", f"ge/{qe}/{ic}")
            if ic >= 4:
                ctx.wait("S", f"hm/{qe}/{ic - 4}")
            elif e > 0:
                ctx.wait("S", f"hm/{q}/{e - 1}/{ic + 2}")
            elif q > 0:
                ctx.wait("S", f"hm/{q - 1}/{NE - 1}/{ic + 2}")
                ctx.wait("S", f"sgl/{q - 1}")
            si = SS.activation(sgb[:, ps_ * TQ:(ps_ + 1) * TQ], ps_g[p], AF.Silu)
            ctx.inc("S", si, key=f"silu/{qe}/{ic}")
        for n in range(8, NST):
            mb["dA"](n)
        mb["prep"]()
        for n in range(5):
            mb["step"](n)
        for ic in range(NI):
            p = ic % 2
            ctx.wait("V", f"ue/{qe}/{ic}")
            ctx.wait("V", f"wbcd/{qe}")
            vi = VV.tensor_tensor(upw[:, p * TQ:(p + 1) * TQ], ps_u[p],
                                  wbc[:, (e % 2) * TQ:(e % 2 + 1) * TQ], ALU.mult)
            ctx.inc("V", vi, key=f"upw/{qe}/{ic}")
            ctx.wait("V", f"silu/{qe}/{ic}")
            if e >= 2:
                ctx.wait("V", f"dn/{q}/{e - 2}/15")
            elif q > 0:
                ctx.wait("V", f"dn/{q - 1}/{e + 30}/15")
            ps_ = ic % 4
            vi = VV.tensor_tensor(
                h_e[:, par_h * NI * TQ + ic * TQ:par_h * NI * TQ + (ic + 1) * TQ],
                upw[:, p * TQ:(p + 1) * TQ], sgb[:, ps_ * TQ:(ps_ + 1) * TQ], ALU.mult)
            ctx.inc("V", vi, key=f"hm/{qe}/{ic}")
            mb["step"](5 + ic)
        nstep = 11
        for hc in range(NCK):
            ctx.wait("V", f"dn/{qe}/{hc}")
            src = dn_bank(hc)
            a_sl = acc[:, hc * TQ:(hc + 1) * TQ]
            if e == 0:
                if q > 0:
                    ctx.wait("V", f"csil_done/{q - 1}")
                vi = VV.tensor_copy(a_sl, src)
            else:
                vi = VV.tensor_add(a_sl, src, a_sl)
            ctx.inc("V", vi, key=f"accv/{qe}/{hc}")
            if hc % 3 == 2 and nstep < NST:
                mb["step"](nstep)
                nstep += 1
        while nstep < NST:
            mb["step"](nstep)
            nstep += 1
        mb["fin"]()
        if e + 1 < NE:
            weight_loads_early(q, e + 1)
        elif q + 1 < NQ:
            weight_loads_early(q + 1, 0)

    def mamba_unit_pre(q, ch, c):
        qc = f"{q}/{ch}/{c}"
        b = ctx.sc_acquire("T")
        ctx.wait("T", f"xproj/{q}/{ch}")
        ti = TT.matmul(ps_sc[:, b * 512:b * 512 + LCH],
                       WdtT[:, c * 128:(c + 1) * 128], xprojb,
                       start=True, stop=True)
        ctx.inc("T", ti, key=f"dtm/{qc}")
        return b

    def mamba_steps(q, ch, c, bank):
        u = ch * 16 + c
        qc = f"{q}/{ch}/{c}"

        def sp():
            ctx.wait("S", f"dtm/{qc}")
            ctx.wait("S", "prologue_s")
            if u >= 1 or q > 0:
                pu, pq = (u - 1, q) if u >= 1 else (31, q - 1)
                ctx.wait("S", f"dtf/{pq}/{pu // 16}/{pu % 16}")
            ps_z = ps_sc[:, bank * 512:bank * 512 + LCH]
            SS.activation(dtc, ps_z, AF.Relu, bias=bdt[:, c:c + 1])
            si = SS.activation(sp0, ps_z, AF.Abs, bias=bdt[:, c:c + 1])
            ctx.inc("S", si, key=f"dtps/{qc}")
            ctx.sc_release(bank, f"dtps/{qc}")
            SS.activation(sp1, sp0, AF.Exp, scale=-1.0)
            si = SS.activation(sp0, sp1, AF.Ln, bias=cpool[:, 1:2])
            ctx.inc("S", si, key=f"sp/{qc}")

        def dA(n):
            ctx.wait("S", f"dtf/{qc}")
            r = n % 8
            if n >= 8:
                ctx.wait("S", f"scan/{qc}/{n - 8}")
            else:
                pu, pq = (u - 1, q) if u >= 1 else ((31, q - 1) if q > 0 else (None, None))
                if pu is not None:
                    ctx.wait("S", f"scan/{pq}/{pu // 16}/{pu % 16}/{n + 8}")
            si = SS.activation(dAr[:, r * LCH:(r + 1) * LCH], dtc,
                               AF.Exp, scale=negP[:, c * NST + n:c * NST + n + 1])
            ctx.inc("S", si, key=f"dA/{qc}/{n}")

        def prep():
            ctx.wait("V", f"sp/{qc}")
            VV.tensor_add(dtc, dtc, sp0)
            vi = VV.tensor_tensor(dtxc, dtc,
                                  xnbf[:, c * TQ + ch * LCH:c * TQ + (ch + 1) * LCH],
                                  ALU.mult)
            ctx.inc("V", vi, key=f"dtf/{qc}")

        def step(n):
            r = n % 8
            ctx.wait("V", f"bc/{q}/{ch}")
            VV.tensor_tensor(dBr[:, (n % 2) * LCH:(n % 2 + 1) * LCH], dtxc,
                             Bbc[:, n * LCH:(n + 1) * LCH], ALU.mult)
            ctx.wait("V", f"dA/{qc}/{n}")
            vi = VV.tensor_tensor_scan(hsc[:, (n % 2) * LCH:(n % 2 + 1) * LCH],
                                       dAr[:, r * LCH:(r + 1) * LCH],
                                       dBr[:, (n % 2) * LCH:(n % 2 + 1) * LCH],
                                       hst[:, c * 16 + n:c * 16 + n + 1],
                                       ALU.mult, ALU.add)
            ctx.inc("V", vi, key=f"scan/{qc}/{n}")
            if n == 0:
                VV.tensor_tensor(y_c, hsc[:, 0:LCH], Cbc[:, 0:LCH], ALU.mult)
            else:
                VV.tensor_tensor(ytmp, hsc[:, (n % 2) * LCH:(n % 2 + 1) * LCH],
                                 Cbc[:, n * LCH:(n + 1) * LCH], ALU.mult)
                VV.tensor_add(y_c, ytmp, y_c)
            VV.tensor_copy(hst[:, c * 16 + n:c * 16 + n + 1],
                           hsc[:, (n % 2) * LCH + LCH - 1:(n % 2) * LCH + LCH])

        def fin():
            sl = u % 2
            if u >= 2:
                pu = u - 2
                ctx.wait("V", f"ssp/{q}/{pu // 16}/{pu % 16}")
            elif q > 0:
                pu = u + 30
                ctx.wait("V", f"ssp/{q - 1}/{pu // 16}/{pu % 16}")
            vi = VV.scalar_tensor_tensor(
                s_c[:, sl * LCH:(sl + 1) * LCH],
                xnbf[:, c * TQ + ch * LCH:c * TQ + (ch + 1) * LCH],
                Dv[:, c:c + 1], y_c, ALU.mult, ALU.add)
            ctx.inc("V", vi, key=f"sdone/{qc}")
            ctx.mark("V", f"unit/{q}/{ch}/{c}")
            ctx.wait("sync", f"sdone/{qc}")
            ctx.dma("sync", f"SS{c % 2}",
                    sfm_d[c * 128:(c + 1) * 128,
                          q * TQ + ch * LCH:q * TQ + (ch + 1) * LCH],
                    s_c[:, sl * LCH:(sl + 1) * LCH], key=f"ssp/{qc}")

        return {"sp": sp, "dA": dA, "prep": prep, "step": step, "fin": fin}

    def combine(q):
        ctx.wait("S", f"accv/{q}/{NE - 1}/15")
        for tt in range(4):
            tile = q * 4 + tt
            ch = tt // 2
            if tt > 0 or q > 0:
                pq, ptt = (q, tt - 1) if tt > 0 else (q - 1, 3)
                ctx.wait("sync", f"odma/{pq}/{ptt}")
            ctx.dma("sync", "SXM", x_tm, x_d[tile * 128:(tile + 1) * 128, :],
                    key=f"xtm/{q}/{tt}")
            ctx.wait("sync", f"ssp/{q}/{ch}/14")
            ctx.wait("sync", f"ssp/{q}/{ch}/15")
            if tt > 0:
                ctx.wait("sync", f"sg/{q}/{tt - 1}/15")
            elif q > 0:
                ctx.wait("sync", f"sg/{q - 1}/3/15")
            ctx.dma("sync", "SL0", s_sl.rearrange("p (c t) -> p c t", c=NCK),
                    sfm_d[:, q * TQ + tt * 128:q * TQ + (tt + 1) * 128].rearrange(
                        "(c p) t -> p c t", p=128), key=f"srl/{q}/{tt}")
            if tt > 0:
                ctx.wait("S", f"sg/{q}/{tt - 1}/15")
            acc3 = acc.rearrange("p (c t) -> p c t", c=NCK)[
                :, :, tt * 128:(tt + 1) * 128]
            si = SS.activation(sgb[:, 0:NCK * 128].rearrange(
                "p (c t) -> p c t", c=NCK), acc3, AF.Silu)
            ctx.inc("S", si, key=f"gsil/{q}/{tt}")
            for c in range(NCK):
                sl2 = c % 2
                ctx.wait("V", f"gsil/{q}/{tt}")
                ctx.wait("V", f"srl/{q}/{tt}")
                if c >= 2:
                    ctx.wait("V", f"ctp/{q}/{tt}/{c - 2}")
                elif tt > 0:
                    ctx.wait("V", f"ctp/{q}/{tt - 1}/{c + 14}")
                vi = VV.tensor_tensor(sg_c[:, sl2 * 128:(sl2 + 1) * 128],
                                      sgb[:, c * 128:(c + 1) * 128],
                                      s_sl[:, c * 128:(c + 1) * 128], ALU.mult)
                ctx.inc("V", vi, key=f"sg/{q}/{tt}/{c}")
                bank = c // 4
                ctx.wait("T", f"sg/{q}/{tt}/{c}")
                if tt == 0 and c == 0:
                    ctx.wait("T", f"accv/{q}/{NE - 1}/14")
                    ctx.wait("T", f"accv/{q}/{NE - 1}/15")
                    ctx.wait("T", f"silu/{q}/31/4")
                    ctx.wait("T", f"silu/{q}/31/5")
                    ctx.wait("T", f"upw/{q}/31/4")
                    ctx.wait("T", f"upw/{q}/31/5")
                if tt > 0:
                    ctx.wait("T", f"oadd/{q}/{tt - 1}/{bank}")
                ti = TT.transpose(ps_cb[bank][:, (c % 4) * 128:(c % 4 + 1) * 128],
                                  sg_c[:, sl2 * 128:(sl2 + 1) * 128], ident)
                ctx.inc("T", ti, key=f"ctp/{q}/{tt}/{c}")
            for g4 in range(4):
                ctx.wait("V", f"ctp/{q}/{tt}/{g4 * 4 + 3}")
                ctx.wait("V", f"xtm/{q}/{tt}")
                vi = VV.tensor_add(x_tm[:, g4 * 512:(g4 + 1) * 512], ps_cb[g4],
                                   x_tm[:, g4 * 512:(g4 + 1) * 512])
                ctx.inc("V", vi, key=f"oadd/{q}/{tt}/{g4}")
            ctx.wait("sync", f"oadd/{q}/{tt}/3")
            ctx.dma("sync", "SO", out_d[tile * 128:(tile + 1) * 128, :], x_tm,
                    key=f"odma/{q}/{tt}")
        ctx.mark("S", f"csil_done/{q}")
        ctx.mark("V", f"sgl/{q}")

    weight_loads_early(0, 0)
    for q in range(NQ):
        chunk_transposes(q, 0)
        chunk_transposes(q, 1)
        router(q)
        chunk_aux(q, 0)
        for e in range(NE):
            if e == 16:
                chunk_aux(q, 1)
            expert_slot(q, e)
        combine(q)
        if dbg and q == 0:
            ctx.wait("sync", f"xnbf/{q}/1/1/15")
            ctx.dma("sync", "SO", dbg_xnbf, xnbf, key="dbg_xnbf")
            ctx.wait("sync", f"accv/{q}/31/15")
            ctx.dma("sync", "SO", dbg_acc, acc, key="dbg_acc")
            ctx.wait("sync", f"w/{q}/3")
            ctx.dma("sync", "SO", dbg_w, w_tm, key="dbg_w")
    if dbg:
        ctx.wait("sync", f"sdone/{NQ - 1}/1/15")
        ctx.dma("sync", "SO", dbg_s, sfm_d, key="dbg_s")
    if ctx.real:
        for sm in ("SO", "SR", "SS0", "SS1", "SBS", "SWT"):
            if ctx.sems[sm].n:
                ctx.eng["sync"].wait_ge(ctx.sems[sm].h, ctx.sems[sm].n)


def build_program(dbg=False):
    ctx1 = Ctx(None)
    _build(ctx1, dbg=dbg)
    nc = bass.Bass("TRN2", target_bir_lowering=False, debug=False,
                   detect_race_conditions=False)
    ctx2 = Ctx(nc, tbl=ctx1.tbl)
    _build(ctx2, dbg=dbg)
    return nc


_CACHE = {}


def _in_maps(inputs):
    x = np.ascontiguousarray(inputs["L0_attn_x"], dtype=np.float32)
    shared = {
        "wx": np.ascontiguousarray(inputs["L0_attn_W_x"], dtype=np.float32),
        "wdt": np.ascontiguousarray(inputs["L0_attn_W_dt"], dtype=np.float32),
        "bdt_in": np.ascontiguousarray(
            inputs["L0_attn_b_dt"], dtype=np.float32).reshape(16, 128),
        "alog": np.ascontiguousarray(inputs["L0_attn_A_log"], dtype=np.float32),
        "dvec_in": np.ascontiguousarray(
            inputs["L0_attn_D"], dtype=np.float32).reshape(16, 128),
        "rw": np.ascontiguousarray(inputs["L0_ffn_router_w"], dtype=np.float32),
        "wgt": np.ascontiguousarray(inputs["L0_ffn_w_gate"], dtype=np.float32),
        "wup": np.ascontiguousarray(inputs["L0_ffn_w_up"], dtype=np.float32),
        "wdn": np.ascontiguousarray(inputs["L0_ffn_w_down"], dtype=np.float32),
        "ident": np.eye(128, dtype=np.float32),
    }
    return [{"x": x[i], **shared} for i in range(8)]


def kernel(**inputs):
    if "nc" not in _CACHE:
        _CACHE["nc"] = build_program()
    res = run_bass_kernel_spmd(_CACHE["nc"], _in_maps(inputs),
                               core_ids=list(range(8)))
    return np.stack([res.results[i]["out"] for i in range(8)], axis=0)
